# version 15
# baseline (speedup 1.0000x reference)
import numpy as np
import concourse.bass as bass
import concourse.mybir as mybir
from concourse.bass_utils import run_bass_kernel_spmd

F32 = mybir.dt.float32
BF16 = mybir.dt.bfloat16
AF = mybir.ActivationFunctionType
OP = mybir.AluOpType

D = 2048
SEQ = 2048
TT = 512
NCH = D // 128
NQKV = 3 * NCH
NHID = 8192 // 128
NHS = 4
HSC = NHID // NHS
NPANELS = NQKV + NCH + NHS * 3 * HSC
SLOTS = 4
SCALE_OUT = float(128.0 ** -0.5)
LN_EPS = 1e-5
ATT_EPS = 1e-9
INV_D = 1.0 / D


def build_program(nt=SEQ // TT):
    nc = bass.Bass("TRN2", target_bir_lowering=False)
    xT = nc.dram_tensor("xT", [D, SEQ], F32, kind="ExternalInput").ap()
    wqkv = nc.dram_tensor("wqkv", [NQKV, 128, D], F32, kind="ExternalInput").ap()
    wout = nc.dram_tensor("wout", [NCH, 128, D], F32, kind="ExternalInput").ap()
    wgate = nc.dram_tensor("wgate", [NHID, 128, D], F32, kind="ExternalInput").ap()
    wup = nc.dram_tensor("wup", [NHID, 128, D], F32, kind="ExternalInput").ap()
    wdown = nc.dram_tensor("wdown", [NHS, NCH, 128, HSC * 128], F32, kind="ExternalInput").ap()
    outT = nc.dram_tensor("outT", [D, SEQ], F32, kind="ExternalOutput").ap()
    wsplit = nc.dram_tensor("wsplit", [NPANELS, 128, 2 * D], BF16, kind="Internal").ap()

    from contextlib import ExitStack
    with ExitStack() as ctx:
        sb = lambda name, shape, dt=F32: ctx.enter_context(nc.sbuf_tensor(name, shape, dt))
        acc = sb("acc", [128, NCH * TT])
        xn = sb("xn", [128, NCH * TT])
        scr = sb("scr", [128, NCH * TT])
        wring = sb("wring", [128, SLOTS * D])
        kphi = sb("kphi", [128, 2 * TT])
        vsb = sb("vsb", [128, 2 * TT])
        qphi = sb("qphi", [128, 2 * TT])
        ptmp = sb("ptmp", [128, 2 * TT])
        kvc = sb("kvc", [128, TT])
        kc = sb("kc", [128, TT])
        tmp1 = sb("tmp1", [128, TT])
        tmp2 = sb("tmp2", [128, TT])
        silu = sb("silu", [128, 2 * TT])
        fstage = sb("fstage", [128, 2 * TT])
        wstage = sb("wstage", [128, 4 * D])
        carries = sb("carries", [128, 2 * NCH])
        ones = sb("ones", [128, 128])
        stats = sb("stats", [1, 4 * TT])
        consts = sb("consts", [128, 8])
        psums = [ctx.enter_context(nc.psum_tensor(f"p{i}", [128, TT], F32))
                 for i in range(8)]
        S_st = ctx.enter_context(nc.semaphore("st"))
        S_mm = ctx.enter_context(nc.semaphore("mm"))
        S_sc = ctx.enter_context(nc.semaphore("sc"))
        S_vc = ctx.enter_context(nc.semaphore("vc"))
        S_wbf = ctx.enter_context(nc.semaphore("wbf"))
        sems = {"st": S_st, "mm": S_mm, "sc": S_sc, "vc": S_vc,
                "wbf": S_wbf}
        for i in range(SLOTS):
            sems[f"wl{i}"] = ctx.enter_context(nc.semaphore(f"wl{i}"))
            sems[f"wb{i}"] = ctx.enter_context(nc.semaphore(f"wb{i}"))
            sems[f"xl{i}"] = ctx.enter_context(nc.semaphore(f"xl{i}"))
        block = ctx.enter_context(nc.Block())

        SCHED = []
        cnt = {"st": 0, "mm": 0, "sc": 0, "vc": 0, "wbf": 0}
        for i in range(SLOTS):
            cnt[f"wl{i}"] = 0
            cnt[f"wb{i}"] = 0
            cnt[f"xl{i}"] = 0

        def ev(engine, fn, waits=(), incs=()):
            SCHED.append((engine, fn, [w for w in waits if w is not None], list(incs)))

        def s_instr(fn, waits=()):
            cnt["sc"] += 1
            ev("scalar", fn, waits, [("sc", 1)])
            return cnt["sc"]

        def v_instr(fn, waits=()):
            cnt["vc"] += 1
            ev("vector", fn, waits, [("vc", 1)])
            return cnt["vc"]

        def ch(buf, c, w=TT):
            return buf[:, c * w:(c + 1) * w]

        xn_ch = [ch(xn, c) for c in range(NCH)]
        acc_ch = [ch(acc, c) for c in range(NCH)]
        scr_ch = [ch(scr, c) for c in range(NCH)]
        xnb = xn[:].bitcast(BF16)
        scrb = scr[:].bitcast(BF16)
        xnb_hi = [xnb[:, c * TT:(c + 1) * TT] for c in range(NCH)]
        xnb_lo = [xnb[:, NCH * TT + c * TT:NCH * TT + (c + 1) * TT] for c in range(NCH)]
        scrb_hi = [scrb[:, c * TT:(c + 1) * TT] for c in range(NCH)]
        scrb_lo = [scrb[:, NCH * TT + c * TT:NCH * TT + (c + 1) * TT] for c in range(NCH)]
        wringb = wring[:].bitcast(BF16)

        assert SLOTS == 4
        panel_seq = [0]
        panel_mm = []
        sub_of = {}
        wb_tgt = {}

        def load_panel(src_f32, first_tile):
            seq = panel_seq[0]
            panel_seq[0] += 1
            j = seq % NPANELS
            slot = seq % SLOTS
            wl = f"wl{slot}"
            if first_tile:
                stg = wstage[:, slot * D:(slot + 1) * D]
                w = []
                if seq >= SLOTS:
                    w.append(("vc", sub_of[seq - SLOTS]))
                cnt[wl] += 16
                tgt = cnt[wl]
                ev("sync", lambda e, d=stg, s=src_f32: e.dma_start(out=d, in_=s),
                   w, [(wl, 16)])
                hi = wringb[:, slot * 2 * D: slot * 2 * D + D]
                lo = wringb[:, slot * 2 * D + D:(slot + 1) * 2 * D]
                cw = [(wl, tgt)]
                if seq >= SLOTS:
                    cw += [("mm", panel_mm[seq - SLOTS]), wb_tgt[seq - SLOTS]]
                cast = s_instr(lambda e, o=hi, i=stg: e.activation(o, i, AF.Copy),
                               cw)
                sub = v_instr(lambda e, o=lo, a=stg, b=hi:
                              e.tensor_tensor(o, a, b, OP.subtract),
                              [("sc", cast)])
                sub_of[seq] = sub
                if j == NPANELS - 1:
                    wbsem = "wbf"
                else:
                    wbsem = f"wb{slot}"
                cnt[wbsem] += 16
                ev("gpsimd",
                   lambda e, d=wsplit[j],
                   s=wringb[:, slot * 2 * D:(slot + 1) * 2 * D]:
                       e.dma_start(out=d, in_=s),
                   [("vc", sub)], [(wbsem, 16)])
                wb_tgt[seq] = (wbsem, cnt[wbsem])
                return slot, ("vc", sub)
            waits = []
            if seq >= SLOTS:
                waits.append(("mm", panel_mm[seq - SLOTS]))
            if seq < 2 * NPANELS:
                waits.append(("wbf", 16))
            dst = wringb[:, slot * 2 * D:(slot + 1) * 2 * D]
            cnt[wl] += 16
            tgt = cnt[wl]
            ev("sync", lambda e, d=dst, s=wsplit[j]: e.dma_start(out=d, in_=s),
               waits, [(wl, 16)])
            return slot, (wl, tgt)

        bank_release = {b: None for b in range(8)}

        def t_group(bank, mms, extra_waits=(), out_ap=None, is_panel=False):
            waits = list(extra_waits)
            if bank_release[bank] is not None:
                waits.append(bank_release[bank])
            out = psums[bank][:] if out_ap is None else out_ap
            n = len(mms)
            for k, mm in enumerate(mms):
                l, r = mm[0], mm[1]
                kw = list(mm[2]) if len(mm) > 2 else []
                first, last = k == 0, k == n - 1
                incs = [("mm", 1)] if last else []
                ev("tensor",
                   lambda e, o=out, l=l, r=r, f=first, la=last:
                       e.matmul(o, l, r, start=f, stop=la),
                   (waits + kw) if first else kw, incs)
            cnt["mm"] += 1
            if is_panel:
                panel_mm.append(cnt["mm"])
            return cnt["mm"]

        def panel_group(bank, slot, ready, rhs_hi, rhs_lo, extra_waits=(),
                        per_k_waits=None):
            base = slot * 2 * D
            mms = []
            for k in range(NCH):
                wh = wringb[:, base + k * 128: base + (k + 1) * 128]
                wl = wringb[:, base + D + k * 128: base + D + (k + 1) * 128]
                kw = per_k_waits[k] if per_k_waits else []
                mms += [(wh, rhs_hi[k], kw), (wh, rhs_lo[k]), (wl, rhs_hi[k])]
            return t_group(bank, mms, list(extra_waits) + [ready],
                           is_panel=True)

        def tile_panel_order():
            o = []
            for c in range(NCH):
                o += [wqkv[NCH + c], wqkv[2 * NCH + c], wqkv[c]]
            o += [wout[oc] for oc in range(NCH)]
            for hs in range(NHS):
                for hc in range(HSC):
                    o += [wgate[hs * HSC + hc], wup[hs * HSC + hc]]
                o += [wdown[hs, oc] for oc in range(NCH)]
            assert len(o) == NPANELS
            return o

        pstate = {}

        def tile_panels_begin(first_tile):
            pstate.update(order=tile_panel_order(), ld=0, use=0,
                          handles={}, first=first_tile)

        def get_panel():
            i = pstate["use"]
            while pstate["ld"] <= min(i + SLOTS - 1, NPANELS - 1):
                pstate["handles"][pstate["ld"]] = load_panel(
                    pstate["order"][pstate["ld"]], pstate["first"])
                pstate["ld"] += 1
            pstate["use"] = i + 1
            return pstate["handles"].pop(i)

        m_sl = stats[0:1, 0:TT]
        msq_sl = stats[0:1, TT:2 * TT]
        var_sl = stats[0:1, 2 * TT:3 * TT]
        rstd_sl = stats[0:1, 3 * TT:4 * TT]

        v_instr(lambda e: e.memset(ones[:], 1.0))
        v_instr(lambda e: e.memset(carries[:], 0.0))
        v_instr(lambda e: e.memset(consts[:, 0:1], -1.0))
        v_instr(lambda e: e.memset(consts[:, 1:2], LN_EPS))
        v_instr(lambda e: e.memset(consts[:, 2:3], -0.5))
        init_vc = v_instr(lambda e: e.memset(consts[:, 3:4], INV_D))
        s_instr(lambda e: e.activation(stats[0:1, 0:1], stats[0:1, 0:1], AF.Ln),
                [("vc", init_vc)])
        c_neg1 = consts[:, 0:1]
        c_eps1 = consts[0:1, 1:2]
        c_mhalf = consts[0:1, 2:3]
        c_invd = consts[0:1, 3:4]

        fill_rot = [0]
        acc_rot = [0]

        def next_fill():
            b = fill_rot[0] % 4
            fill_rot[0] += 1
            return b

        def next_acc():
            b = 4 + acc_rot[0] % 2
            acc_rot[0] += 1
            return b

        rel = {name: [0, 0] for name in ("kphi", "vsb", "qphi", "ptmp", "silu")}
        fst_cast = [0, 0]

        last_mm_scr_readers = 0
        def fsplit(p, dst_hi, dst_lo, vwrite):
            cast = s_instr(lambda e, o=dst_hi, i=fstage[:, p * TT:(p + 1) * TT]:
                           e.activation(o, i, AF.Copy), [("vc", vwrite)])
            fst_cast[p] = cast
            return v_instr(lambda e, o=dst_lo, a=fstage[:, p * TT:(p + 1) * TT],
                           b=dst_hi: e.tensor_tensor(o, a, b, OP.subtract),
                           [("sc", cast)])

        for t in range(nt):
            t0 = t * TT
            tile_panels_begin(t == 0)

            xq_w = []
            for q in range(4):
                sem = f"xl{q}"
                cnt[sem] += 16
                ev("gpsimd",
                   lambda e,
                   d=acc[:].rearrange("p (g j) -> p g j", j=TT)[:, 4 * q:4 * q + 4, :],
                   s=xT.rearrange("(g p) T -> p g T", p=128)[:, 4 * q:4 * q + 4, t0:t0 + TT]:
                       e.dma_start(out=d, in_=s),
                   (), [(sem, 16)])
                xq_w.append((sem, cnt[sem]))
            xld_w = xq_w[-1]

            sq_idx = []
            for c in range(NCH):
                w = [xq_w[c // 4]] if c % 4 == 0 else []
                if c == 0:
                    w.append(("mm", last_mm_scr_readers))
                sq_idx.append(s_instr(
                    lambda e, o=scr_ch[c], i=acc_ch[c]: e.activation(o, i, AF.Square),
                    w))
            g_sum = t_group(6, [(ones[:, 0:1], acc_ch[c], [xq_w[c // 4]])
                                for c in range(NCH)],
                            [("vc", init_vc)], out_ap=psums[6][0:1, :])
            g_sq = t_group(7, [(ones[:, 0:1], scr_ch[c], [("sc", sq_idx[c])])
                               for c in range(NCH)],
                           [], out_ap=psums[7][0:1, :])
            m_idx = s_instr(lambda e: e.activation(m_sl, psums[6][0:1, :],
                                                   AF.Copy, scale=c_invd),
                            [("mm", g_sum)])
            bank_release[6] = ("sc", m_idx)
            msq_idx = s_instr(lambda e: e.activation(msq_sl, psums[7][0:1, :],
                                                     AF.Copy, scale=c_invd),
                              [("mm", g_sq)])
            bank_release[7] = ("sc", msq_idx)
            v_instr(lambda e: e.tensor_mul(var_sl, m_sl, m_sl), [("sc", msq_idx)])
            v2 = v_instr(lambda e: e.tensor_sub(var_sl, msq_sl, var_sl))
            s_instr(lambda e: e.activation(rstd_sl, var_sl, AF.Ln, bias=c_eps1),
                    [("vc", v2)])
            r2 = s_instr(lambda e: e.activation(rstd_sl, rstd_sl, AF.Exp,
                                                scale=c_mhalf))
            g_mb = t_group(6, [(ones[0:1, 0:128], m_sl)], [("sc", m_idx)])
            g_rb = t_group(7, [(ones[0:1, 0:128], rstd_sl)], [("sc", r2)])
            vx = None
            xn_ready = []
            for c in range(NCH):
                p = c % 2
                fs = fstage[:, p * TT:(p + 1) * TT]
                w = [("mm", g_mb)] if c == 0 else []
                v_instr(lambda e, o=fs, i=acc_ch[c]:
                        e.tensor_sub(o, i, psums[6][:]),
                        w + [("sc", fst_cast[p])])
                wm = [("mm", g_rb)] if c == 0 else []
                vm = v_instr(lambda e, o=fs: e.tensor_mul(o, o, psums[7][:]), wm)
                vx = fsplit(p, xnb_hi[c], xnb_lo[c], vm)
                xn_ready.append(vx)
            xn_done = vx
            bank_release[6] = ("vc", vx)
            bank_release[7] = ("vc", vx)

            combine_last = None
            scr_ready = []
            for c in range(NCH):
                p = c % 2
                kslot = ch(kphi, p)
                vslot = ch(vsb, p)
                qslot = ch(qphi, p)
                pslot = ch(ptmp, p)
                kvcs = kvc[:]
                kcs = kc[:]
                t1s = tmp1[:]
                t2s = tmp2[:]

                bank_k = next_fill()
                slot_k, rdy_k = get_panel()
                pkw = ([[("vc", xn_ready[k])] for k in range(NCH)]
                       if c == 0 else None)
                gk = panel_group(bank_k, slot_k, rdy_k, xnb_hi, xnb_lo,
                                 per_k_waits=pkw)
                s_instr(lambda e, o=pslot, b=bank_k:
                        e.activation(o, psums[b][:], AF.Relu, scale=c_neg1),
                        [("mm", gk), ("vc", rel["ptmp"][p]), ("vc", rel["kphi"][p])])
                s2 = s_instr(lambda e, o=kslot, b=bank_k:
                             e.activation(o, psums[b][:], AF.Relu))
                bank_release[bank_k] = ("sc", s2)
                s3 = s_instr(lambda e, o=pslot:
                             e.activation(o, o, AF.Exp, scale=c_neg1))
                va = v_instr(lambda e, o=kslot, i=pslot:
                             e.tensor_add(o, o, i), [("sc", s3)])

                bank_v = next_fill()
                slot_v, rdy_v = get_panel()
                gv = panel_group(bank_v, slot_v, rdy_v, xnb_hi, xnb_lo)
                sv = s_instr(lambda e, o=vslot, b=bank_v:
                             e.copy(o, psums[b][:]),
                             [("mm", gv), ("vc", rel["vsb"][p])])
                bank_release[bank_v] = ("sc", sv)

                v_instr(lambda e, o=vslot, i=kslot: e.tensor_mul(o, o, i),
                        [("sc", sv)])
                vs1 = v_instr(lambda e, o=kvcs, i=vslot, init=carries[:, c:c + 1]:
                              e.tensor_tensor_scan(o, i, i, init, OP.add, OP.bypass))
                vs3 = v_instr(lambda e, o=kcs, i=kslot, init=carries[:, NCH + c:NCH + c + 1]:
                              e.tensor_tensor_scan(o, i, i, init, OP.add, OP.bypass))
                rel["vsb"][p] = vs1
                rel["kphi"][p] = vs3

                bank_q = next_fill()
                slot_q, rdy_q = get_panel()
                gq = panel_group(bank_q, slot_q, rdy_q, xnb_hi, xnb_lo)
                s_instr(lambda e, o=pslot, b=bank_q:
                        e.activation(o, psums[b][:], AF.Relu, scale=c_neg1),
                        [("mm", gq), ("vc", va)])
                s2q = s_instr(lambda e, o=qslot, b=bank_q:
                              e.activation(o, psums[b][:], AF.Relu),
                              [("vc", rel["qphi"][p])])
                bank_release[bank_q] = ("sc", s2q)
                s3q = s_instr(lambda e, o=pslot:
                              e.activation(o, o, AF.Exp, scale=c_neg1))
                vqa = v_instr(lambda e, o=qslot, i=pslot: e.tensor_add(o, o, i),
                              [("sc", s3q)])
                rel["ptmp"][p] = vqa

                v_instr(lambda e, o=t1s, a=qslot, b=kvcs: e.tensor_mul(o, a, b))
                vc2 = v_instr(lambda e, o=t2s, a=qslot, b=kcs: e.tensor_mul(o, a, b))
                rel["qphi"][p] = vc2
                v_instr(lambda e, o=carries[:, c:c + 1], i=kvcs[:, TT - 1:TT]:
                        e.tensor_copy(o, i))
                v_instr(lambda e, o=carries[:, NCH + c:NCH + c + 1], i=kcs[:, TT - 1:TT]:
                        e.tensor_copy(o, i))
                v_instr(lambda e, o=t2s: e.tensor_scalar_add(o, o, ATT_EPS))
                v_instr(lambda e, o=t2s: e.reciprocal(o, o))
                fs = fstage[:, p * TT:(p + 1) * TT]
                vm = v_instr(lambda e, o=fs, a=t1s, b=t2s: e.tensor_mul(o, a, b),
                             [("sc", fst_cast[p])])
                combine_last = fsplit(p, scrb_hi[c], scrb_lo[c], vm)
                scr_ready.append(combine_last)

            vacc = None
            g = None
            for oc in range(NCH):
                bank = next_acc()
                slot, rdy = get_panel()
                pkw = ([[("vc", scr_ready[k])] for k in range(NCH)]
                       if oc == 0 else None)
                g = panel_group(bank, slot, rdy, scrb_hi, scrb_lo,
                                per_k_waits=pkw)
                vacc = v_instr(
                    lambda e, o=acc_ch[oc], b=bank:
                        e.scalar_tensor_tensor(o, psums[b][:], SCALE_OUT, o,
                                               OP.mult, OP.add),
                    [("mm", g)])
                bank_release[bank] = ("vc", vacc)
            last_wout_mm = g

            prev_scr_readers = last_wout_mm
            last_down = None
            for hs in range(NHS):
                gated_done = None
                gated_ready = []
                for hc in range(HSC):
                    p = hc % 2
                    bank_g = next_fill()
                    slot_g, rdy_g = get_panel()
                    gg = panel_group(bank_g, slot_g, rdy_g, xnb_hi, xnb_lo)
                    bank_u = next_fill()
                    slot_u, rdy_u = get_panel()
                    gu = panel_group(bank_u, slot_u, rdy_u, xnb_hi, xnb_lo)
                    sslot = ch(silu, p)
                    ss = s_instr(lambda e, o=sslot, b=bank_g:
                                 e.activation(o, psums[b][:], AF.Silu),
                                 [("mm", gg), ("vc", rel["silu"][p])])
                    bank_release[bank_g] = ("sc", ss)
                    w2 = [("sc", ss), ("mm", gu), ("mm", prev_scr_readers)]
                    fs = fstage[:, p * TT:(p + 1) * TT]
                    vm = v_instr(lambda e, o=fs, a=sslot, b=bank_u:
                                 e.tensor_mul(o, a, psums[b][:]),
                                 w2 + [("sc", fst_cast[p])])
                    bank_release[bank_u] = ("vc", vm)
                    rel["silu"][p] = vm
                    gated_done = fsplit(p, scrb_hi[hc], scrb_lo[hc], vm)
                    gated_ready.append(gated_done)
                last_hs = hs == NHS - 1
                if last_hs:
                    s_instr(lambda e: e.activation(var_sl[0:1, 0:1],
                                                   var_sl[0:1, 0:1], AF.Ln))
                for oc in range(NCH):
                    bank = next_acc()
                    slot, rdy = get_panel()
                    pkw = ([[("vc", gated_ready[k])] for k in range(HSC)]
                           if oc == 0 else None)
                    last_down = panel_group(bank, slot, rdy, scrb_hi, scrb_lo,
                                            per_k_waits=pkw)
                    vacc = v_instr(
                        lambda e, o=acc_ch[oc], b=bank:
                            e.tensor_add(o, o, psums[b][:]),
                        [("mm", last_down)])
                    bank_release[bank] = ("vc", vacc)
                    if last_hs:
                        cnt["st"] += 16
                        ev("gpsimd",
                           lambda e,
                           d=outT[oc * 128:(oc + 1) * 128, t0:t0 + TT],
                           s=acc_ch[oc]: e.dma_start(out=d, in_=s),
                           [("vc", vacc)], [("st", 16)])
                prev_scr_readers = last_down
            acc_done = vacc
            last_mm_scr_readers = last_down

        final_st = cnt["st"]

        def make_body(name):
            def body(eng):
                hw = {}
                for engine, fn, waits, incs in SCHED:
                    if engine != name:
                        continue
                    for s, v in waits:
                        if v > hw.get(s, 0):
                            eng.wait_ge(sems[s], v)
                            hw[s] = v
                    inst = fn(eng) if fn is not None else None
                    for i, (s, n) in enumerate(incs):
                        if i == 0 and inst is not None:
                            inst.then_inc(sems[s], n)
                        else:
                            eng.nop().then_inc(sems[s], n)
            return body

        block.sync(make_body("sync"))
        block.tensor(make_body("tensor"))
        block.vector(make_body("vector"))
        block.scalar(make_body("scalar"))

        def gp_body(eng):
            make_body("gpsimd")(eng)
            eng.wait_ge(S_st, final_st)
            eng.wait_ge(S_wbf, cnt["wbf"])
        block.gpsimd(gp_body)

    return nc



def pack_lhsT(W):
    W = np.ascontiguousarray(np.asarray(W, dtype=np.float32))
    od, idim = W.shape
    noc, nk = od // 128, idim // 128
    A = W.reshape(noc, 128, nk, 128).transpose(0, 3, 2, 1)
    return np.ascontiguousarray(A).reshape(noc, 128, idim)


def pack_down(W):
    W = np.ascontiguousarray(np.asarray(W, dtype=np.float32))
    A = W.reshape(NCH, 128, NHS, HSC, 128).transpose(2, 0, 4, 3, 1)
    return np.ascontiguousarray(A).reshape(NHS, NCH, 128, HSC * 128)


def kernel(L0_attn_x, L0_attn_W_qkv, L0_attn_W_out, L0_ffn_w_gate, L0_ffn_w_up,
           L0_ffn_w_down):
    x = np.asarray(L0_attn_x, dtype=np.float32)
    B = x.shape[0]
    pq = pack_lhsT(L0_attn_W_qkv)
    po = pack_lhsT(L0_attn_W_out)
    pg = pack_lhsT(L0_ffn_w_gate)
    pu = pack_lhsT(L0_ffn_w_up)
    pd = pack_down(L0_ffn_w_down)

    nc = build_program()
    in_maps = [
        {
            "xT": np.ascontiguousarray(x[c].T),
            "wqkv": pq, "wout": po, "wgate": pg, "wup": pu, "wdown": pd,
        }
        for c in range(B)
    ]
    res = run_bass_kernel_spmd(nc, in_maps, core_ids=list(range(B)))
    out = np.stack([np.ascontiguousarray(res.results[c]["outT"]).T
                    for c in range(B)])
    return out.astype(np.float32)


# version 16
# speedup vs baseline: 1.1656x; 1.1656x over previous
import numpy as np
import concourse.bass as bass
import concourse.mybir as mybir
from concourse.bass_utils import run_bass_kernel_spmd

F32 = mybir.dt.float32
BF16 = mybir.dt.bfloat16
AF = mybir.ActivationFunctionType
OP = mybir.AluOpType

D = 2048
SEQ = 2048
TT = 512
NCH = D // 128
NQKV = 3 * NCH
NHID = 8192 // 128
NHS = 4
HSC = NHID // NHS
NPANELS = NQKV + NCH + NHS * 3 * HSC
SLOTS = 4
SCALE_OUT = float(128.0 ** -0.5)
LN_EPS = 1e-5
ATT_EPS = 1e-9
INV_D = 1.0 / D


def build_program(nt=SEQ // TT):
    nc = bass.Bass("TRN2", target_bir_lowering=False)
    xT = nc.dram_tensor("xT", [D, SEQ], F32, kind="ExternalInput").ap()
    wqkv = nc.dram_tensor("wqkv", [NQKV, 128, D], F32, kind="ExternalInput").ap()
    wout = nc.dram_tensor("wout", [NCH, 128, D], F32, kind="ExternalInput").ap()
    wgate = nc.dram_tensor("wgate", [NHID, 128, D], F32, kind="ExternalInput").ap()
    wup = nc.dram_tensor("wup", [NHID, 128, D], F32, kind="ExternalInput").ap()
    wdown = nc.dram_tensor("wdown", [NHS, NCH, 128, HSC * 128], F32, kind="ExternalInput").ap()
    outT = nc.dram_tensor("outT", [D, SEQ], F32, kind="ExternalOutput").ap()
    wsplit = nc.dram_tensor("wsplit", [NPANELS, 128, 2 * D], BF16, kind="Internal").ap()

    from contextlib import ExitStack
    with ExitStack() as ctx:
        sb = lambda name, shape, dt=F32: ctx.enter_context(nc.sbuf_tensor(name, shape, dt))
        acc = sb("acc", [128, NCH * TT])
        xn = sb("xn", [128, NCH * TT])
        scr = sb("scr", [128, NCH * TT])
        wring = sb("wring", [128, SLOTS * D])
        kphi = sb("kphi", [128, 2 * TT])
        vsb = sb("vsb", [128, 2 * TT])
        qphi = sb("qphi", [128, 2 * TT])
        ptmp = sb("ptmp", [128, 2 * TT])
        kvc = sb("kvc", [128, TT])
        kc = sb("kc", [128, TT])
        tmp1 = sb("tmp1", [128, TT])
        tmp2 = sb("tmp2", [128, TT])
        silu = sb("silu", [128, 2 * TT])
        fstage = sb("fstage", [128, 2 * TT])
        wstage = sb("wstage", [128, 4 * D])
        carries = sb("carries", [128, 2 * NCH])
        ones = sb("ones", [128, 128])
        stats = sb("stats", [1, 4 * TT])
        consts = sb("consts", [128, 8])
        psums = [ctx.enter_context(nc.psum_tensor(f"p{i}", [128, TT], F32))
                 for i in range(8)]
        S_st = ctx.enter_context(nc.semaphore("st"))
        S_mm = ctx.enter_context(nc.semaphore("mm"))
        S_sc = ctx.enter_context(nc.semaphore("sc"))
        S_vc = ctx.enter_context(nc.semaphore("vc"))
        S_wbf = ctx.enter_context(nc.semaphore("wbf"))
        sems = {"st": S_st, "mm": S_mm, "sc": S_sc, "vc": S_vc,
                "wbf": S_wbf}
        for i in range(SLOTS):
            sems[f"wl{i}"] = ctx.enter_context(nc.semaphore(f"wl{i}"))
            sems[f"wb{i}"] = ctx.enter_context(nc.semaphore(f"wb{i}"))
            sems[f"xl{i}"] = ctx.enter_context(nc.semaphore(f"xl{i}"))
        block = ctx.enter_context(nc.Block())

        SCHED = []
        cnt = {"st": 0, "mm": 0, "sc": 0, "vc": 0, "wbf": 0}
        for i in range(SLOTS):
            cnt[f"wl{i}"] = 0
            cnt[f"wb{i}"] = 0
            cnt[f"xl{i}"] = 0

        def ev(engine, fn, waits=(), incs=()):
            SCHED.append((engine, fn, [w for w in waits if w is not None], list(incs)))

        def s_instr(fn, waits=()):
            cnt["sc"] += 1
            ev("scalar", fn, waits, [("sc", 1)])
            return cnt["sc"]

        def v_instr(fn, waits=()):
            cnt["vc"] += 1
            ev("vector", fn, waits, [("vc", 1)])
            return cnt["vc"]

        def ch(buf, c, w=TT):
            return buf[:, c * w:(c + 1) * w]

        xn_ch = [ch(xn, c) for c in range(NCH)]
        acc_ch = [ch(acc, c) for c in range(NCH)]
        scr_ch = [ch(scr, c) for c in range(NCH)]
        xnb = xn[:].bitcast(BF16)
        scrb = scr[:].bitcast(BF16)
        xnb_hi = [xnb[:, c * TT:(c + 1) * TT] for c in range(NCH)]
        xnb_lo = [xnb[:, NCH * TT + c * TT:NCH * TT + (c + 1) * TT] for c in range(NCH)]
        scrb_hi = [scrb[:, c * TT:(c + 1) * TT] for c in range(NCH)]
        scrb_lo = [scrb[:, NCH * TT + c * TT:NCH * TT + (c + 1) * TT] for c in range(NCH)]
        wringb = wring[:].bitcast(BF16)

        assert SLOTS == 4
        panel_seq = [0]
        panel_mm = []
        sub_of = {}
        wb_tgt = {}

        def load_panel(src_f32, first_tile):
            seq = panel_seq[0]
            panel_seq[0] += 1
            j = seq % NPANELS
            slot = seq % SLOTS
            wl = f"wl{slot}"
            if first_tile:
                stg = wstage[:, slot * D:(slot + 1) * D]
                w = []
                if seq >= SLOTS:
                    w.append(("vc", sub_of[seq - SLOTS]))
                cnt[wl] += 16
                tgt = cnt[wl]
                ev("sync", lambda e, d=stg, s=src_f32: e.dma_start(out=d, in_=s),
                   w, [(wl, 16)])
                hi = wringb[:, slot * 2 * D: slot * 2 * D + D]
                lo = wringb[:, slot * 2 * D + D:(slot + 1) * 2 * D]
                cw = [(wl, tgt)]
                if seq >= SLOTS:
                    cw += [("mm", panel_mm[seq - SLOTS]), wb_tgt[seq - SLOTS]]
                cast = s_instr(lambda e, o=hi, i=stg: e.activation(o, i, AF.Copy),
                               cw)
                sub = v_instr(lambda e, o=lo, a=stg, b=hi:
                              e.tensor_tensor(o, a, b, OP.subtract),
                              [("sc", cast)])
                sub_of[seq] = sub
                if j == NPANELS - 1:
                    wbsem = "wbf"
                else:
                    wbsem = f"wb{slot}"
                cnt[wbsem] += 16
                ev("gpsimd",
                   lambda e, d=wsplit[j],
                   s=wringb[:, slot * 2 * D:(slot + 1) * 2 * D]:
                       e.dma_start(out=d, in_=s),
                   [("vc", sub)], [(wbsem, 16)])
                wb_tgt[seq] = (wbsem, cnt[wbsem])
                return slot, ("vc", sub)
            waits = []
            if seq >= SLOTS:
                waits.append(("mm", panel_mm[seq - SLOTS]))
            if seq < 2 * NPANELS:
                waits.append(("wbf", 16))
            dst = wringb[:, slot * 2 * D:(slot + 1) * 2 * D]
            cnt[wl] += 16
            tgt = cnt[wl]
            ev("sync", lambda e, d=dst, s=wsplit[j]: e.dma_start(out=d, in_=s),
               waits, [(wl, 16)])
            return slot, (wl, tgt)

        bank_release = {b: None for b in range(8)}

        def t_group(bank, mms, extra_waits=(), out_ap=None, is_panel=False):
            waits = list(extra_waits)
            if bank_release[bank] is not None:
                waits.append(bank_release[bank])
            out = psums[bank][:] if out_ap is None else out_ap
            n = len(mms)
            for k, mm in enumerate(mms):
                l, r = mm[0], mm[1]
                kw = list(mm[2]) if len(mm) > 2 else []
                first, last = k == 0, k == n - 1
                incs = [("mm", 1)] if last else []
                ev("tensor",
                   lambda e, o=out, l=l, r=r, f=first, la=last:
                       e.matmul(o, l, r, start=f, stop=la),
                   (waits + kw) if first else kw, incs)
            cnt["mm"] += 1
            if is_panel:
                panel_mm.append(cnt["mm"])
            return cnt["mm"]

        def panel_group(bank, slot, ready, rhs_hi, rhs_lo, extra_waits=(),
                        per_k_waits=None):
            base = slot * 2 * D
            mms = []
            for k in range(NCH):
                wh = wringb[:, base + k * 128: base + (k + 1) * 128]
                wl = wringb[:, base + D + k * 128: base + D + (k + 1) * 128]
                kw = per_k_waits[k] if per_k_waits else []
                mms += [(wh, rhs_hi[k], kw), (wh, rhs_lo[k]), (wl, rhs_hi[k])]
            return t_group(bank, mms, list(extra_waits) + [ready],
                           is_panel=True)

        def tile_panel_order():
            o = []
            for c in range(NCH):
                o += [wqkv[NCH + c], wqkv[2 * NCH + c], wqkv[c]]
            o += [wout[oc] for oc in range(NCH)]
            for hs in range(NHS):
                for hc in range(HSC):
                    o += [wgate[hs * HSC + hc], wup[hs * HSC + hc]]
                o += [wdown[hs, oc] for oc in range(NCH)]
            assert len(o) == NPANELS
            return o

        pstate = {}

        def tile_panels_begin(first_tile):
            pstate.update(order=tile_panel_order(), ld=0, use=0,
                          handles={}, first=first_tile)

        def get_panel():
            i = pstate["use"]
            while pstate["ld"] <= min(i + SLOTS - 1, NPANELS - 1):
                pstate["handles"][pstate["ld"]] = load_panel(
                    pstate["order"][pstate["ld"]], pstate["first"])
                pstate["ld"] += 1
            pstate["use"] = i + 1
            return pstate["handles"].pop(i)

        m_sl = stats[0:1, 0:TT]
        msq_sl = stats[0:1, TT:2 * TT]
        var_sl = stats[0:1, 2 * TT:3 * TT]
        rstd_sl = stats[0:1, 3 * TT:4 * TT]

        v_instr(lambda e: e.memset(ones[:], 1.0))
        v_instr(lambda e: e.memset(carries[:], 0.0))
        v_instr(lambda e: e.memset(consts[:, 0:1], -1.0))
        v_instr(lambda e: e.memset(consts[:, 1:2], LN_EPS))
        v_instr(lambda e: e.memset(consts[:, 2:3], -0.5))
        init_vc = v_instr(lambda e: e.memset(consts[:, 3:4], INV_D))
        s_instr(lambda e: e.activation(stats[0:1, 0:1], stats[0:1, 0:1], AF.Ln))
        c_neg1 = consts[:, 0:1]
        c_eps1 = consts[0:1, 1:2]
        c_mhalf = consts[0:1, 2:3]
        c_invd = consts[0:1, 3:4]

        fill_rot = [0]
        acc_rot = [0]

        def next_fill():
            b = fill_rot[0] % 4
            fill_rot[0] += 1
            return b

        def next_acc():
            b = 4 + acc_rot[0] % 2
            acc_rot[0] += 1
            return b

        rel = {name: [0, 0] for name in ("kphi", "vsb", "qphi", "ptmp", "silu")}
        fst_cast = [0, 0]

        last_mm_scr_readers = 0
        def fsplit(p, dst_hi, dst_lo, vwrite):
            cast = s_instr(lambda e, o=dst_hi, i=fstage[:, p * TT:(p + 1) * TT]:
                           e.activation(o, i, AF.Copy), [("vc", vwrite)])
            fst_cast[p] = cast
            return v_instr(lambda e, o=dst_lo, a=fstage[:, p * TT:(p + 1) * TT],
                           b=dst_hi: e.tensor_tensor(o, a, b, OP.subtract),
                           [("sc", cast)])

        for t in range(nt):
            t0 = t * TT
            tile_panels_begin(t == 0)

            xq_w = []
            for q in range(4):
                sem = f"xl{q}"
                cnt[sem] += 16
                ev("gpsimd",
                   lambda e,
                   d=acc[:].rearrange("p (g j) -> p g j", j=TT)[:, 4 * q:4 * q + 4, :],
                   s=xT.rearrange("(g p) T -> p g T", p=128)[:, 4 * q:4 * q + 4, t0:t0 + TT]:
                       e.dma_start(out=d, in_=s),
                   (), [(sem, 16)])
                xq_w.append((sem, cnt[sem]))
            xld_w = xq_w[-1]

            sq_idx = []
            for c in range(NCH):
                w = [xq_w[c // 4]] if c % 4 == 0 else []
                if c == 0:
                    w.append(("mm", last_mm_scr_readers))
                sq_idx.append(s_instr(
                    lambda e, o=scr_ch[c], i=acc_ch[c]: e.activation(o, i, AF.Square),
                    w))
            g_sum = t_group(6, [(ones[:, 0:1], acc_ch[c], [xq_w[c // 4]])
                                for c in range(NCH)],
                            [("vc", init_vc)], out_ap=psums[6][0:1, :])
            g_sq = t_group(7, [(ones[:, 0:1], scr_ch[c], [("sc", sq_idx[c])])
                               for c in range(NCH)],
                           [], out_ap=psums[7][0:1, :])
            m_idx = s_instr(lambda e: e.activation(m_sl, psums[6][0:1, :],
                                                   AF.Copy, scale=c_invd),
                            [("mm", g_sum)])
            bank_release[6] = ("sc", m_idx)
            msq_idx = s_instr(lambda e: e.activation(msq_sl, psums[7][0:1, :],
                                                     AF.Copy, scale=c_invd),
                              [("mm", g_sq)])
            bank_release[7] = ("sc", msq_idx)
            v_instr(lambda e: e.tensor_mul(var_sl, m_sl, m_sl), [("sc", msq_idx)])
            v2 = v_instr(lambda e: e.tensor_sub(var_sl, msq_sl, var_sl))
            s_instr(lambda e: e.activation(rstd_sl, var_sl, AF.Ln, bias=c_eps1),
                    [("vc", v2)])
            r2 = s_instr(lambda e: e.activation(rstd_sl, rstd_sl, AF.Exp,
                                                scale=c_mhalf))
            g_mb = t_group(6, [(ones[0:1, 0:128], m_sl)], [("sc", m_idx)])
            g_rb = t_group(7, [(ones[0:1, 0:128], rstd_sl)], [("sc", r2)])
            vx = None
            xn_ready = []
            for c in range(NCH):
                p = c % 2
                fs = fstage[:, p * TT:(p + 1) * TT]
                w = [("mm", g_mb)] if c == 0 else []
                v_instr(lambda e, o=fs, i=acc_ch[c]:
                        e.tensor_sub(o, i, psums[6][:]),
                        w + [("sc", fst_cast[p])])
                wm = [("mm", g_rb)] if c == 0 else []
                vm = v_instr(lambda e, o=fs: e.tensor_mul(o, o, psums[7][:]), wm)
                vx = fsplit(p, xnb_hi[c], xnb_lo[c], vm)
                xn_ready.append(vx)
            xn_done = vx
            bank_release[6] = ("vc", vx)
            bank_release[7] = ("vc", vx)

            combine_last = None
            scr_ready = []
            for c in range(NCH):
                p = c % 2
                kslot = ch(kphi, p)
                vslot = ch(vsb, p)
                qslot = ch(qphi, p)
                pslot = ch(ptmp, p)
                kvcs = kvc[:]
                kcs = kc[:]
                t1s = tmp1[:]
                t2s = tmp2[:]

                bank_k = next_fill()
                slot_k, rdy_k = get_panel()
                pkw = ([[("vc", xn_ready[k])] for k in range(NCH)]
                       if c == 0 else None)
                gk = panel_group(bank_k, slot_k, rdy_k, xnb_hi, xnb_lo,
                                 per_k_waits=pkw)
                s_instr(lambda e, o=pslot, b=bank_k:
                        e.activation(o, psums[b][:], AF.Relu, scale=c_neg1),
                        [("mm", gk), ("vc", rel["ptmp"][p]), ("vc", rel["kphi"][p])])
                s2 = s_instr(lambda e, o=kslot, b=bank_k:
                             e.activation(o, psums[b][:], AF.Relu))
                bank_release[bank_k] = ("sc", s2)
                s3 = s_instr(lambda e, o=pslot:
                             e.activation(o, o, AF.Exp, scale=c_neg1))
                va = v_instr(lambda e, o=kslot, i=pslot:
                             e.tensor_add(o, o, i), [("sc", s3)])

                bank_v = next_fill()
                slot_v, rdy_v = get_panel()
                gv = panel_group(bank_v, slot_v, rdy_v, xnb_hi, xnb_lo)
                sv = s_instr(lambda e, o=vslot, b=bank_v:
                             e.copy(o, psums[b][:]),
                             [("mm", gv), ("vc", rel["vsb"][p])])
                bank_release[bank_v] = ("sc", sv)

                v_instr(lambda e, o=vslot, i=kslot: e.tensor_mul(o, o, i),
                        [("sc", sv)])
                vs1 = v_instr(lambda e, o=kvcs, i=vslot, init=carries[:, c:c + 1]:
                              e.tensor_tensor_scan(o, i, i, init, OP.add, OP.bypass))
                vs3 = v_instr(lambda e, o=kcs, i=kslot, init=carries[:, NCH + c:NCH + c + 1]:
                              e.tensor_tensor_scan(o, i, i, init, OP.add, OP.bypass))
                rel["vsb"][p] = vs1
                rel["kphi"][p] = vs3

                bank_q = next_fill()
                slot_q, rdy_q = get_panel()
                gq = panel_group(bank_q, slot_q, rdy_q, xnb_hi, xnb_lo)
                s_instr(lambda e, o=pslot, b=bank_q:
                        e.activation(o, psums[b][:], AF.Relu, scale=c_neg1),
                        [("mm", gq), ("vc", va)])
                s2q = s_instr(lambda e, o=qslot, b=bank_q:
                              e.activation(o, psums[b][:], AF.Relu),
                              [("vc", rel["qphi"][p])])
                bank_release[bank_q] = ("sc", s2q)
                s3q = s_instr(lambda e, o=pslot:
                              e.activation(o, o, AF.Exp, scale=c_neg1))
                vqa = v_instr(lambda e, o=qslot, i=pslot: e.tensor_add(o, o, i),
                              [("sc", s3q)])
                rel["ptmp"][p] = vqa

                v_instr(lambda e, o=t1s, a=qslot, b=kvcs: e.tensor_mul(o, a, b))
                vc2 = v_instr(lambda e, o=t2s, a=qslot, b=kcs: e.tensor_mul(o, a, b))
                rel["qphi"][p] = vc2
                v_instr(lambda e, o=carries[:, c:c + 1], i=kvcs[:, TT - 1:TT]:
                        e.tensor_copy(o, i))
                v_instr(lambda e, o=carries[:, NCH + c:NCH + c + 1], i=kcs[:, TT - 1:TT]:
                        e.tensor_copy(o, i))
                v_instr(lambda e, o=t2s: e.tensor_scalar_add(o, o, ATT_EPS))
                v_instr(lambda e, o=t2s: e.reciprocal(o, o))
                fs = fstage[:, p * TT:(p + 1) * TT]
                vm = v_instr(lambda e, o=fs, a=t1s, b=t2s: e.tensor_mul(o, a, b),
                             [("sc", fst_cast[p])])
                combine_last = fsplit(p, scrb_hi[c], scrb_lo[c], vm)
                scr_ready.append(combine_last)

            vacc = None
            g = None
            for oc in range(NCH):
                bank = next_acc()
                slot, rdy = get_panel()
                pkw = ([[("vc", scr_ready[k])] for k in range(NCH)]
                       if oc == 0 else None)
                g = panel_group(bank, slot, rdy, scrb_hi, scrb_lo,
                                per_k_waits=pkw)
                vacc = v_instr(
                    lambda e, o=acc_ch[oc], b=bank:
                        e.scalar_tensor_tensor(o, psums[b][:], SCALE_OUT, o,
                                               OP.mult, OP.add),
                    [("mm", g)])
                bank_release[bank] = ("vc", vacc)
            last_wout_mm = g

            prev_scr_readers = last_wout_mm
            last_down = None
            for hs in range(NHS):
                gated_done = None
                gated_ready = []
                for hc in range(HSC):
                    p = hc % 2
                    bank_g = next_fill()
                    slot_g, rdy_g = get_panel()
                    gg = panel_group(bank_g, slot_g, rdy_g, xnb_hi, xnb_lo)
                    bank_u = next_fill()
                    slot_u, rdy_u = get_panel()
                    gu = panel_group(bank_u, slot_u, rdy_u, xnb_hi, xnb_lo)
                    sslot = ch(silu, p)
                    ss = s_instr(lambda e, o=sslot, b=bank_g:
                                 e.activation(o, psums[b][:], AF.Silu),
                                 [("mm", gg), ("vc", rel["silu"][p])])
                    bank_release[bank_g] = ("sc", ss)
                    w2 = [("sc", ss), ("mm", gu), ("mm", prev_scr_readers)]
                    fs = fstage[:, p * TT:(p + 1) * TT]
                    vm = v_instr(lambda e, o=fs, a=sslot, b=bank_u:
                                 e.tensor_mul(o, a, psums[b][:]),
                                 w2 + [("sc", fst_cast[p])])
                    bank_release[bank_u] = ("vc", vm)
                    rel["silu"][p] = vm
                    gated_done = fsplit(p, scrb_hi[hc], scrb_lo[hc], vm)
                    gated_ready.append(gated_done)
                last_hs = hs == NHS - 1
                if last_hs:
                    s_instr(lambda e: e.activation(var_sl[0:1, 0:1],
                                                   var_sl[0:1, 0:1], AF.Ln))
                for oc in range(NCH):
                    bank = next_acc()
                    slot, rdy = get_panel()
                    pkw = ([[("vc", gated_ready[k])] for k in range(HSC)]
                           if oc == 0 else None)
                    last_down = panel_group(bank, slot, rdy, scrb_hi, scrb_lo,
                                            per_k_waits=pkw)
                    vacc = v_instr(
                        lambda e, o=acc_ch[oc], b=bank:
                            e.tensor_add(o, o, psums[b][:]),
                        [("mm", last_down)])
                    bank_release[bank] = ("vc", vacc)
                    if last_hs:
                        cnt["st"] += 16
                        ev("gpsimd",
                           lambda e,
                           d=outT[oc * 128:(oc + 1) * 128, t0:t0 + TT],
                           s=acc_ch[oc]: e.dma_start(out=d, in_=s),
                           [("vc", vacc)], [("st", 16)])
                prev_scr_readers = last_down
            acc_done = vacc
            last_mm_scr_readers = last_down

        final_st = cnt["st"]

        def make_body(name):
            def body(eng):
                hw = {}
                for engine, fn, waits, incs in SCHED:
                    if engine != name:
                        continue
                    for s, v in waits:
                        if v > hw.get(s, 0):
                            eng.wait_ge(sems[s], v)
                            hw[s] = v
                    inst = fn(eng) if fn is not None else None
                    for i, (s, n) in enumerate(incs):
                        if i == 0 and inst is not None:
                            inst.then_inc(sems[s], n)
                        else:
                            eng.nop().then_inc(sems[s], n)
            return body

        block.sync(make_body("sync"))
        block.tensor(make_body("tensor"))
        block.vector(make_body("vector"))
        block.scalar(make_body("scalar"))

        def gp_body(eng):
            make_body("gpsimd")(eng)
            eng.wait_ge(S_st, final_st)
            eng.wait_ge(S_wbf, cnt["wbf"])
        block.gpsimd(gp_body)

    return nc



def pack_lhsT(W):
    W = np.ascontiguousarray(np.asarray(W, dtype=np.float32))
    od, idim = W.shape
    noc, nk = od // 128, idim // 128
    A = W.reshape(noc, 128, nk, 128).transpose(0, 3, 2, 1)
    return np.ascontiguousarray(A).reshape(noc, 128, idim)


def pack_down(W):
    W = np.ascontiguousarray(np.asarray(W, dtype=np.float32))
    A = W.reshape(NCH, 128, NHS, HSC, 128).transpose(2, 0, 4, 3, 1)
    return np.ascontiguousarray(A).reshape(NHS, NCH, 128, HSC * 128)


def kernel(L0_attn_x, L0_attn_W_qkv, L0_attn_W_out, L0_ffn_w_gate, L0_ffn_w_up,
           L0_ffn_w_down):
    x = np.asarray(L0_attn_x, dtype=np.float32)
    B = x.shape[0]
    pq = pack_lhsT(L0_attn_W_qkv)
    po = pack_lhsT(L0_attn_W_out)
    pg = pack_lhsT(L0_ffn_w_gate)
    pu = pack_lhsT(L0_ffn_w_up)
    pd = pack_down(L0_ffn_w_down)

    nc = build_program()
    in_maps = [
        {
            "xT": np.ascontiguousarray(x[c].T),
            "wqkv": pq, "wout": po, "wgate": pg, "wup": pu, "wdown": pd,
        }
        for c in range(B)
    ]
    res = run_bass_kernel_spmd(nc, in_maps, core_ids=list(range(B)))
    out = np.stack([np.ascontiguousarray(res.results[c]["outT"]).T
                    for c in range(B)])
    return out.astype(np.float32)
